# version 07
# baseline (speedup 1.0000x reference)
import numpy as np
import ml_dtypes
import concourse.bass as bass
import concourse.mybir as mybir
from concourse.bass_utils import run_bass_kernel_spmd

F32 = mybir.dt.float32
BF16 = mybir.dt.bfloat16
I32 = mybir.dt.int32
I8 = mybir.dt.int8
AL = mybir.AluOpType
ACT = mybir.ActivationFunctionType
AX = mybir.AxisListType

D = 1024
E = 2048
N = 16
LOWN = 2048
LG = 4096
EPS = 1e-5
MAGIC = float(np.float32(1.5 * 2 ** 23))
PAIRS = [[0, 1], [2, 3], [4, 5], [6, 7]]
ALL8 = [[0, 1, 2, 3, 4, 5, 6, 7]]




_NO_RECORD = {"wait_ge", "wait_op", "maybe_wait_ge", "sem_inc", "drain", "nop",
              "value_load", "reg_load", "load", "sem_clear"}


class _EngProxy:

    def __init__(self):
        self._eng = None
        self._last = None

    def _bind(self, eng):
        self._eng = eng
        return self

    def __getattr__(self, name):
        eng = object.__getattribute__(self, "_eng")
        f = getattr(eng, name)
        if not callable(f) or name in _NO_RECORD or name.startswith("_"):
            return f

        def wrap(*a, **k):
            r = f(*a, **k)
            if hasattr(r, "then_inc"):
                object.__setattr__(self, "_last", r)
            return r
        return wrap

    def wait_ge(self, *a, **k):
        return self._eng.wait_ge(*a, **k)

    def sem_inc(self, sem, val):
        if val != 1:
            return self._eng.sem_inc(sem, val)
        last = self._last
        if last is not None and not last.ins.has_update():
            try:
                return last.then_inc(sem, val)
            except AssertionError:
                pass
        return self._eng.drain(fusable=False).then_inc(sem, val)


class _BlockProxy:
    def __init__(self, block):
        self._block = block
        self._proxies = {n: _EngProxy() for n in
                         ("sync", "vector", "scalar", "tensor", "gpsimd")}

    def _mk(self, name):
        dec = getattr(self._block, name)
        prox = self._proxies[name]

        def my_dec(f):
            @dec
            def _(eng):
                f(prox._bind(eng))
        return my_dec

    @property
    def sync(self):
        return self._mk("sync")

    @property
    def vector(self):
        return self._mk("vector")

    @property
    def scalar(self):
        return self._mk("scalar")

    @property
    def tensor(self):
        return self._mk("tensor")

    @property
    def gpsimd(self):
        return self._mk("gpsimd")


def build_nc():
    import os
    PH = int(os.environ.get("KPH", "5"))
    P0T = int(os.environ.get("KP0T", "17"))
    P0X = int(os.environ.get("KP0X", "1"))
    P0S = int(os.environ.get("KP0S", "1"))
    P0G = int(os.environ.get("KP0G", "1"))
    P0E = int(os.environ.get("KP0E", "1"))
    P1U = int(os.environ.get("KP1U", "1"))
    P1B = int(os.environ.get("KP1B", "1"))
    P1C = int(os.environ.get("KP1C", "1"))
    P1X = int(os.environ.get("KP1X", "1"))
    P1S = int(os.environ.get("KP1S", "1"))
    KOUT = int(os.environ.get("KOUT", "0"))
    nc = bass.Bass("TRN2", target_bir_lowering=False, detect_race_conditions=False)

    x3 = nc.dram_tensor("x3", [LOWN + 3, D], F32, kind="ExternalInput").ap()
    winu = nc.dram_tensor("winu", [16, 8, 128, 128], F32, kind="ExternalInput").ap()
    winz = nc.dram_tensor("winz", [8, 8, 128, 128], F32, kind="ExternalInput").ap()
    wxp = nc.dram_tensor("wxp", [16, 128, 96], F32, kind="ExternalInput").ap()
    wdt = nc.dram_tensor("wdt", [8, 64, 128], F32, kind="ExternalInput").ap()
    wout = nc.dram_tensor("wout", [16, 128, D], F32, kind="ExternalInput").ap()
    wqt = nc.dram_tensor("wqt", [8, 128, D], I8, kind="ExternalInput").ap()
    alog_c = nc.dram_tensor("alog_c", [128, 128], F32, kind="ExternalInput").ap()
    bdt_c = nc.dram_tensor("bdt_c", [128, 8], F32, kind="ExternalInput").ap()
    dpar_c = nc.dram_tensor("dpar_c", [128, 8], F32, kind="ExternalInput").ap()
    convw_c = nc.dram_tensor("convw_c", [128, 64], F32, kind="ExternalInput").ap()
    convb_c = nc.dram_tensor("convb_c", [128, 16], F32, kind="ExternalInput").ap()
    lng_c = nc.dram_tensor("lng_c", [128, 8], F32, kind="ExternalInput").ap()
    lnb_c = nc.dram_tensor("lnb_c", [128, 8], F32, kind="ExternalInput").ap()
    wsc_c = nc.dram_tensor("wsc_c", [128, 8], F32, kind="ExternalInput").ap()
    fb_c = nc.dram_tensor("fb_c", [128, 8], F32, kind="ExternalInput").ap()
    res_c = nc.dram_tensor("res_c", [128, 1], F32, kind="ExternalInput").ap()
    pfx_c = nc.dram_tensor("pfx_c", [128, 1], F32, kind="ExternalInput").ap()
    hsc = nc.dram_tensor("hsc", [1, 1], I32, kind="ExternalInput").ap()
    idc = nc.dram_tensor("idc", [128, 128], BF16, kind="ExternalInput").ap()
    rsel = nc.dram_tensor("rsel", [32, 32 * 128], BF16, kind="ExternalInput").ap()
    out_hbm = nc.dram_tensor("out", [LOWN, D], F32, kind="ExternalOutput").ap()

    xn2t_ag_in = nc.dram_tensor("xn2t_ag_in", [D, LOWN], BF16)
    xn2t_pref = nc.dram_tensor("xn2t_pref", [D, 16], BF16)
    xn2t_ag_out = nc.dram_tensor("xn2t_ag_out", [2 * D, LOWN], BF16)
    u2lo = nc.dram_tensor("u2lo", [1024, LOWN], BF16)
    u2hi = nc.dram_tensor("u2hi", [1024, LOWN], BF16)
    u2ag = nc.dram_tensor("u2ag", [4096, LOWN], BF16)
    xdbl_in = nc.dram_tensor("xdbl_in", [96, LOWN], F32)
    xdbl_out = nc.dram_tensor("xdbl_out", [192, LOWN], F32)
    ya_in = nc.dram_tensor("ya_in", [1024, LOWN], BF16)
    yb_in = nc.dram_tensor("yb_in", [1024, LOWN], BF16)
    yag = nc.dram_tensor("yag", [2 * E, LOWN], BF16)
    u2my = nc.dram_tensor("u2my", [2048, LOWN], BF16)
    ymy = nc.dram_tensor("ymy", [2048, LOWN], BF16)
    g_dram = nc.dram_tensor("g_dram", [LOWN, D], BF16)
    gmax_in = nc.dram_tensor("gmax_in", [1, 8], F32)
    gmax_out = nc.dram_tensor("gmax_out", [1, 8], F32)
    col_b = nc.dram_tensor("col_b", [128, 1], F32)

    from contextlib import ExitStack
    with ExitStack() as stk:
        en = stk.enter_context

        iden = en(nc.sbuf_tensor("iden", [128, 128], BF16))
        ones_bf = en(nc.sbuf_tensor("ones_bf", [1, 128], BF16))
        rowsel = en(nc.sbuf_tensor("rowsel", [32, 32 * 128], BF16))
        a_sc = en(nc.sbuf_tensor("a_sc", [128, 128], F32))
        sb_bdt = en(nc.sbuf_tensor("sb_bdt", [128, 8], F32))
        sb_dpar = en(nc.sbuf_tensor("sb_dpar", [128, 8], F32))
        sb_convw = en(nc.sbuf_tensor("sb_convw", [128, 64], F32))
        sb_convb = en(nc.sbuf_tensor("sb_convb", [128, 16], F32))
        sb_lng = en(nc.sbuf_tensor("sb_lng", [128, 8], F32))
        sb_lnbb = en(nc.sbuf_tensor("sb_lnbb", [128, 10], BF16))
        sb_wsc = en(nc.sbuf_tensor("sb_wsc", [128, 8], F32))
        sb_fb = en(nc.sbuf_tensor("sb_fb", [128, 8], F32))
        sb_res = en(nc.sbuf_tensor("sb_res", [128, 1], F32))
        sb_pfx = en(nc.sbuf_tensor("sb_pfx", [128, 1], F32))
        sb_hsc = en(nc.sbuf_tensor("sb_hsc", [1, 1], I32))

        negmu = en(nc.sbuf_tensor("negmu", [128, 16], F32))
        rs1s = en(nc.sbuf_tensor("rs1s", [128, 16], F32))
        rs2s = en(nc.sbuf_tensor("rs2s", [128, 16], F32))
        pf_negmu = en(nc.sbuf_tensor("pf_negmu", [3, 1], F32))
        pf_rs2 = en(nc.sbuf_tensor("pf_rs2", [3, 1], F32))
        gmax = en(nc.sbuf_tensor("gmax", [128, 1], F32))
        sc_misc = en(nc.sbuf_tensor("sc_misc", [1, 16], F32))
        sc_bf = en(nc.sbuf_tensor("sc_bf", [1, 8], BF16))
        invs_c = en(nc.sbuf_tensor("invs_c", [128, 1], F32))
        xs_c = en(nc.sbuf_tensor("xs_c", [128, 1], F32))
        sb_wscx = en(nc.sbuf_tensor("sb_wscx", [128, 8], F32))
        bias_u = en(nc.sbuf_tensor("bias_u", [128, 2], F32))
        scr3 = en(nc.sbuf_tensor("scr3", [128, 4], F32))
        kcol = en(nc.sbuf_tensor("kcol", [128, 2], F32))
        bz_all = en(nc.sbuf_tensor("bz_all", [128, 8], F32))
        hst = en(nc.sbuf_tensor("hst", [128, 128], F32))
        p0c = en(nc.sbuf_tensor("p0c", [128, 12], F32))

        dt_all = en(nc.sbuf_tensor("dt_all", [128, 8 * 1024], F32))
        v_all = en(nc.sbuf_tensor("v_all", [128, 8 * 1024], F32))
        y_all = en(nc.sbuf_tensor("y_all", [128, 8 * 1024], F32))
        sz_all = en(nc.sbuf_tensor("sz_all", [128, 8 * 1024], BF16))
        wz_bf = en(nc.sbuf_tensor("wz_bf", [128, 8 * 1024], BF16))
        xn2tg = en(nc.sbuf_tensor("xn2tg", [128, 8 * 1024], BF16))
        dtlo_bf = [en(nc.sbuf_tensor(f"dtlo_bf{q}", [64, LOWN], BF16)) for q in range(2)]
        bc_bf = [en(nc.sbuf_tensor(f"bc_bf{q}", [32, LOWN], BF16)) for q in range(2)]
        wdt_bf = en(nc.sbuf_tensor("wdt_bf", [64, 8 * 128], BF16))
        u2sc = en(nc.sbuf_tensor("u2sc", [128, 1024], BF16))
        sp_tmp = en(nc.sbuf_tensor("sp_tmp", [128, 1024], F32))
        dA_t = [en(nc.sbuf_tensor(f"dA{q}", [128, 1024], F32)) for q in range(2)]
        w_t = en(nc.sbuf_tensor("w_t", [128, 1024], F32))
        h_t = en(nc.sbuf_tensor("h_t", [128, 1024], F32))
        hc_t = en(nc.sbuf_tensor("hc_t", [128, 1024], F32))
        yec_bf = en(nc.sbuf_tensor("yec_bf", [128, 1024], BF16))

        dtb = dt_all[:].bitcast(BF16)
        vab = v_all[:].bitcast(BF16)
        yab = y_all[:].bitcast(BF16)
        wzf = wz_bf[:].bitcast(F32)

        p0x = dt_all[:, 0:1024]
        p0t = dt_all[:, 1024:2048]
        p0sq = dt_all[:, 2048:3072]
        p0b = vab[:, 0:1024]
        p0stage = vab[:, 2048:3072]
        xmov = [dtb[:, 0:8224], vab[:, 0:8224]]
        wu_bf = [yab[:, 0:1024], yab[:, 1024:2048]]
        wxp_bf = yab[:, 2048:3584]
        u_ec = sz_all[:, 0:2052]
        u2_ec = sz_all[:, 2560:4608]
        diag_k = sz_all[:, 5120:5632]
        xdbl_sb = wzf[0:96, 0:2048]
        wq_bf = yab[:, 0:8192]
        fqT = dtb[:, 0:4096]
        fq = dtb[:, 4096:5120]
        fg = dtb[:, 5632:6144]
        fgT = dtb[:, 6656:7680]
        fx = v_all[:, 0:1024]
        wo_bf = dtb[:, 0:16384]
        yst = yab[:, 0:8192]
        as_x = v_all[:, 0:1024]
        as_xn = v_all[:, 1024:2048]
        as_s = v_all[:, 2048:3072]
        as_o = [v_all[:, 3072:4096], v_all[:, 4096:5120]]
        as_g = sz_all[:, 0:1024]

        psA = en(nc.psum_tensor("psA", [128, 512], F32))
        psB = en(nc.psum_tensor("psB", [128, 512], F32))
        psC = en(nc.psum_tensor("psC", [128, 512], F32))
        psD = en(nc.psum_tensor("psD", [128, 512], F32))
        psBig = en(nc.psum_tensor("psBig", [128, 2048], F32))
        psCb = psC[:].bitcast(BF16)
        psDb = psD[:].bitcast(BF16)

        sems = {}
        cnt = {}
        for name in ["din", "dgi", "dsp", "dou", "sv", "ss", "st", "sg", "cc", "bar"]:
            sems[name] = en(nc.semaphore(name))
            cnt[name] = 0

        def nv(name):
            cnt[name] += 1
            return cnt[name]

        def nd(name):
            cnt[name] += 16
            return cnt[name]

        block = _BlockProxy(en(nc.Block()))
        nbar = [0]

        def barrier():
            nbar[0] += 1
            target = 5 * nbar[0]
            for dec in (block.sync, block.vector, block.scalar, block.tensor,
                        block.gpsimd):
                @dec
                def _(e_):
                    e_.sem_inc(sems["bar"], 1)
                    e_.wait_ge(sems["bar"], target)

        @block.sync
        def _(sync):
            for dst, src in [(a_sc, alog_c), (sb_bdt, bdt_c), (sb_dpar, dpar_c),
                             (sb_convw, convw_c), (sb_convb, convb_c),
                             (sb_lng, lng_c), (sb_wsc, wsc_c), (sb_fb, fb_c),
                             (sb_res, res_c), (sb_pfx, pfx_c), (sb_hsc, hsc)]:
                sync.dma_start(out=dst[:], in_=src).then_inc(sems["din"], 16)
        cnt["din"] += 11 * 16
        d_c0 = cnt["din"]

        @block.gpsimd
        def _(g):
            g.dma_start(out=sb_lnbb[:, 0:8], in_=lnb_c).then_inc(sems["dgi"], 16)
            g.dma_start(out=iden[:], in_=idc).then_inc(sems["dgi"], 16)
            g.dma_start(out=rowsel[:], in_=rsel).then_inc(sems["dgi"], 16)
        cnt["dgi"] += 48
        d_cg = cnt["dgi"]

        @block.vector
        def _(v):
            v.memset(ones_bf[:], 1.0)
            v.memset(gmax[:], 0.0)
            v.memset(hst[:], 0.0)
            v.memset(dt_all[:], 0.0)
            v.memset(v_all[:], 0.0)
            v.memset(y_all[:], 0.0)
            v.memset(sz_all[:], 0.0)
            v.memset(wz_bf[:], 0.0)
            v.memset(xn2tg[:], 0.0)
            v.wait_ge(sems["dgi"], d_cg)
            v.tensor_copy(sb_lnbb[:, 8:10], sb_lnbb[:, 0:2])
            v.sem_inc(sems["sv"], 1)
        v_c0 = nv("sv")

        @block.scalar
        def _(s):
            s.wait_ge(sems["din"], d_c0)
            s.wait_ge(sems["dgi"], d_cg)
            s.activation(a_sc[:], a_sc[:], ACT.Exp).then_inc(sems["ss"], 1)
        s_aexp = nv("ss")

        @block.vector
        def _(v):
            v.wait_ge(sems["ss"], s_aexp)
            v.tensor_scalar(a_sc[:], a_sc[:], -1.0, None, AL.mult).then_inc(
                sems["sv"], 1)
        nv("sv")

        barrier()

        RUN_P0 = PH >= 1
        RUN_P1A = PH >= 2
        RUN_FFN = PH >= 3
        RUN_SCAN = PH >= 4
        RUN_WOUT = PH >= 5
        last_stage_dma = [0]
        last_x_use = [0]

        def p0_tile(t):
            rows = 3 if t == 0 else 128
            r0 = 0 if t == 0 else 3 + (t - 1) * 128

            @block.sync
            def _(sync):
                if t >= 1:
                    sync.wait_ge(sems["sv"], last_x_use[0])
                sync.dma_start(out=p0x[0:rows, :], in_=x3[r0:r0 + rows, :]).then_inc(
                    sems["din"], 16)
            dv = nd("din")

            @block.scalar
            def _(s):
                s.wait_ge(sems["din"], dv)
                s.activation(p0sq[0:rows, :], p0x[0:rows, :], ACT.Square).then_inc(
                    sems["ss"], 1)
            sqv = nv("ss")

            nm = pf_negmu if t == 0 else negmu[:, t - 1:t]
            rs2c = pf_rs2 if t == 0 else rs2s[:, t - 1:t]

            @block.vector
            def _(v):
                v.wait_ge(sems["din"], dv)
                v.reduce_sum(p0c[0:rows, 0:1], p0x[0:rows, :], AX.X)
                v.drain(fusable=False)
                v.tensor_scalar(nm[0:rows, 0:1], p0c[0:rows, 0:1], -1.0 / D, None,
                                AL.mult)
                v.wait_ge(sems["ss"], sqv)
                v.reduce_sum(p0c[0:rows, 1:2], p0sq[0:rows, :], AX.X)
                v.drain(fusable=False)
                v.tensor_scalar(p0c[0:rows, 2:3], p0c[0:rows, 1:2], 1.0 / D, None,
                                AL.mult)
                v.tensor_scalar(p0c[0:rows, 3:4], nm[0:rows, 0:1], nm[0:rows, 0:1],
                                None, AL.mult)
                v.drain(fusable=False)
                v.tensor_sub(p0c[0:rows, 4:5], p0c[0:rows, 2:3], p0c[0:rows, 3:4])
                v.drain(fusable=False)
                v.tensor_scalar(p0c[0:rows, 5:6], p0c[0:rows, 4:5], 1.0 + EPS,
                                EPS * EPS, AL.mult, AL.add)
                v.drain(fusable=False)
                v.reciprocal(p0c[0:rows, 5:6], p0c[0:rows, 5:6])
                v.tensor_scalar(p0c[0:rows, 6:7], p0c[0:rows, 4:5], EPS, None, AL.add)
                v.drain(fusable=False)
                v.reciprocal(p0c[0:rows, 6:7], p0c[0:rows, 6:7])
                v.sem_inc(sems["sv"], 1)
            vv = nv("sv")

            @block.scalar
            def _(s):
                s.wait_ge(sems["sv"], vv)
                s.activation(rs2c[0:rows, 0:1], p0c[0:rows, 5:6], ACT.Sqrt)
                tgt = p0c[0:rows, 7:8] if t == 0 else rs1s[0:rows, t - 1:t]
                s.activation(tgt, p0c[0:rows, 6:7], ACT.Sqrt)
                s.sem_inc(sems["ss"], 1)
            ssv = nv("ss")

            @block.vector
            def _(v):
                v.wait_ge(sems["ss"], ssv)
                v.tensor_scalar(p0t[0:rows, :], p0x[0:rows, :], nm[0:rows, 0:1],
                                None, AL.add)
                v.tensor_scalar(p0b[0:rows, :], p0t[0:rows, :], rs2c[0:rows, 0:1],
                                None, AL.mult)
                if t > 0:
                    v.tensor_reduce(p0c[0:rows, 8:9], p0t[0:rows, :], AX.X, AL.max,
                                    apply_absolute_value=True)
                    v.drain(fusable=False)
                    v.tensor_mul(p0c[0:rows, 9:10], p0c[0:rows, 8:9],
                                 rs1s[0:rows, t - 1:t])
                    v.drain(fusable=False)
                    v.tensor_max(gmax[0:rows, :], gmax[0:rows, :], p0c[0:rows, 9:10])
                v.sem_inc(sems["sv"], 1)
            bv = nv("sv")
            last_x_use[0] = bv

            if P0X:
                @block.tensor
                def _(te):
                    te.wait_ge(sems["sv"], bv)
                    if p0_prev_ev[0]:
                        te.wait_ge(sems["sv"], p0_prev_ev[0])
                    for dc in range(8):
                        te.transpose(psCb[0:128, dc * 128:(dc + 1) * 128],
                                     p0b[:, dc * 128:(dc + 1) * 128],
                                     iden[:]).then_inc(
                            sems["st"], 1 if dc == 7 else 0)
                cnt["st"] += 1
                tp_done = cnt["st"]

                @block.vector
                def _(v):
                    v.wait_ge(sems["st"], tp_done)
                    if last_stage_dma[0] > 0:
                        v.wait_ge(sems["dsp"], last_stage_dma[0])
                    if P0E:
                        for dc in range(8):
                            v.tensor_scalar(p0stage[:, dc * 128:(dc + 1) * 128],
                                            psCb[0:128, dc * 128:(dc + 1) * 128],
                                            sb_lng[:, dc:dc + 1], None, AL.mult)
                    v.sem_inc(sems["sv"], 1)
                p0_prev_ev[0] = nv("sv")
            evals = [cnt["sv"]]

            @block.sync
            def _(sync):
                if not (P0X and P0S):
                    sync.sem_inc(sems["dsp"], 16)
                    return
                sync.wait_ge(sems["sv"], p0_prev_ev[0])
                if t == 0:
                    sync.dma_start(
                        out=xn2t_pref.ap().rearrange("(dc p) c -> p dc c", dc=8)[:, :, 4:8],
                        in_=p0stage[:].rearrange("p (dc c) -> p dc c", dc=8)[:, :, 0:4],
                    ).then_inc(sems["dsp"], 16)
                else:
                    sync.dma_start(
                        out=xn2t_ag_in.ap().rearrange("(dc p) c -> p dc c", dc=8)
                        [:, :, (t - 1) * 128:t * 128],
                        in_=p0stage[:].rearrange("p (dc c) -> p dc c", dc=8)
                        [:, :, 0:128],
                    ).then_inc(sems["dsp"], 16)
            last_stage_dma[0] = nd("dsp")

        p0_prev_ev = [0]
        for t in range(P0T if RUN_P0 else 0):
            p0_tile(t)
        p0_done_sv = cnt["sv"]
        p0_done_dsp = cnt["dsp"]

        @block.gpsimd
        def _(g):
            g.wait_ge(sems["dsp"], p0_done_dsp)
            g.collective_compute("AllGather", AL.bypass, replica_groups=PAIRS,
                                 ins=[xn2t_ag_in.ap().opt()],
                                 outs=[xn2t_ag_out.ap().opt()]).then_inc(sems["cc"], 1)
        cc_xn2t = nv("cc")

        @block.sync
        def _(sync):
            if RUN_P0 and P0G:
                sync.wait_ge(sems["sv"], p0_done_sv)
                sync.dma_start(out=col_b.ap(), in_=gmax[:]).then_inc(sems["dsp"], 16)
                sync.wait_ge(sems["dsp"], nd("dsp"))
                sync.dma_start(out=p0t[0:1, 0:128],
                               in_=col_b.ap().rearrange("(a p) one -> a (p one)", a=1)
                               ).then_inc(sems["din"], 16)
            else:
                sync.sem_inc(sems["dsp"], 16)
                nd("dsp")
                sync.sem_inc(sems["din"], 16)
        d_gm = nd("din")

        @block.vector
        def _(v):
            if RUN_P0 and P0G:
                v.wait_ge(sems["din"], d_gm)
                v.tensor_reduce(sc_misc[0:1, 0:1], p0t[0:1, 0:128], AX.X, AL.max)
                v.drain(fusable=False)
                v.memset(sc_misc[0:1, 1:8], 0.0)
            v.sem_inc(sems["sv"], 1)
        v_gm = nv("sv")

        @block.sync
        def _(sync):
            sync.wait_ge(sems["sv"], v_gm)
            if RUN_P0 and P0G:
                sync.dma_start(out=gmax_in.ap(), in_=sc_misc[0:1, 0:8]).then_inc(
                    sems["dsp"], 16)
            else:
                sync.sem_inc(sems["dsp"], 16)
        d_gmi = nd("dsp")

        @block.gpsimd
        def _(g):
            g.wait_ge(sems["dsp"], d_gmi)
            g.collective_compute("AllReduce", AL.max, replica_groups=ALL8,
                                 ins=[gmax_in.ap().opt()],
                                 outs=[gmax_out.ap().opt()]).then_inc(sems["cc"], 1)
        cc_max = nv("cc")

        @block.sync
        def _(sync):
            sync.wait_ge(sems["cc"], cc_max)
            if P0G:
                sync.dma_start(out=sc_misc[0:1, 8:16], in_=gmax_out.ap()).then_inc(
                    sems["din"], 16)
            else:
                sync.sem_inc(sems["din"], 16)
        d_gmo = nd("din")

        @block.vector
        def _(v):
            v.wait_ge(sems["din"], d_gmo)
            if not P0G:
                v.sem_inc(sems["sv"], 1)
                return
            v.tensor_scalar(sc_misc[0:1, 1:2], sc_misc[0:1, 8:9], 1.0 / 127.0, 1e-5,
                            AL.mult, AL.max)
            v.drain(fusable=False)
            v.reciprocal(sc_misc[0:1, 2:3], sc_misc[0:1, 1:2])
            v.drain(fusable=False)
            v.tensor_copy(sc_bf[0:1, 0:2], sc_misc[0:1, 1:3])
            v.drain(fusable=False)
            v.tensor_copy(sc_misc[0:1, 4:6], sc_bf[0:1, 0:2])
            v.drain(fusable=False)
            v.tensor_sub(sc_misc[0:1, 6:8], sc_misc[0:1, 1:3], sc_misc[0:1, 4:6])
            v.drain(fusable=False)
            v.tensor_copy(sc_bf[0:1, 2:4], sc_misc[0:1, 6:8])
            v.sem_inc(sems["sv"], 1)
        v_scl = nv("sv")

        @block.tensor
        def _(te):
            te.wait_ge(sems["sv"], v_scl)
            if P0G:
                te.matmul(psC[:, 0:4], ones_bf[:], sc_bf[0:1, 0:4], start=True,
                          stop=True).then_inc(sems["st"], 1)
            else:
                te.sem_inc(sems["st"], 1)
        st_rep = nv("st")

        @block.vector
        def _(v):
            v.wait_ge(sems["st"], st_rep)
            if not P0G:
                v.sem_inc(sems["sv"], 1)
                return
            v.tensor_copy(p0c[:, 0:4], psC[:, 0:4])
            v.drain(fusable=False)
            v.tensor_add(xs_c[:], p0c[:, 0:1], p0c[:, 2:3])
            v.tensor_add(invs_c[:], p0c[:, 1:2], p0c[:, 3:4])
            v.drain(fusable=False)
            v.tensor_scalar(sb_wscx[:], sb_wsc[:], xs_c[:, 0:1], None, AL.mult)
            v.sem_inc(sems["sv"], 1)
        v_scl2 = nv("sv")

        barrier()

        @block.sync
        def _(sync):
            if not RUN_P1A:
                return
            sync.wait_ge(sems["dsp"], p0_done_dsp)
            for s in range(2):
                for dc in range(8):
                    if s == 0:
                        sync.dma_start(out=xmov[0][:, dc * 1028:dc * 1028 + 3],
                                       in_=xn2t_pref.ap()[dc * 128:(dc + 1) * 128, 4:7]
                                       ).then_inc(sems["din"], 16)
                        sync.dma_start(out=xmov[0][:, dc * 1028 + 3:dc * 1028 + 1027],
                                       in_=xn2t_ag_in.ap()[dc * 128:(dc + 1) * 128, 0:1024]
                                       ).then_inc(sems["din"], 16)
                    else:
                        sync.dma_start(out=xmov[1][:, dc * 1028:dc * 1028 + 1027],
                                       in_=xn2t_ag_in.ap()[dc * 128:(dc + 1) * 128, 1021:2048]
                                       ).then_inc(sems["din"], 16)
        if RUN_P1A:
            cnt["din"] += 16 * 24
        d_xmov = cnt["din"]

        @block.gpsimd
        def _(g):
            if not RUN_P1A:
                return
            for ec in range(16):
                g.dma_start(out=wxp_bf[:, ec * 96:(ec + 1) * 96], in_=wxp[ec]).then_inc(
                    sems["dgi"], 16)
        if RUN_P1A:
            cnt["dgi"] += 16 * 16
        d_wxp = cnt["dgi"]

        st_u_last = {}
        st_conv_last = [0]
        sv_uev = {}
        ss_silu = {}
        sv_bias = {}
        dsp_u2 = {}
        st_wxp = {}
        rounds = [(0, 512), (512, 512), (1024, 512), (1536, 512), (2048, 4)]

        for ec in range(16 if RUN_P1A else 0):
            par = ec % 2

            @block.gpsimd
            def _(g, ec=ec, par=par):
                if ec >= 2:
                    g.wait_ge(sems["st"], st_u_last[ec - 2])
                g.dma_start(out=wu_bf[par][:].rearrange("p (kc c) -> p kc c", kc=8),
                            in_=winu[ec].rearrange("kc p c -> p kc c")).then_inc(
                    sems["dgi"], 16)
            d_wu = nd("dgi")

            @block.tensor
            def _(te, ec=ec, par=par, d_wu=d_wu):
                te.wait_ge(sems["dgi"], d_wu)
                if ec == 0:
                    te.wait_ge(sems["din"], d_xmov)
                    te.wait_ge(sems["sv"], v_scl2)
                else:
                    te.wait_ge(sems["ss"], ss_silu[(ec - 1, 2)])
                if P1B:
                    for kc in range(8):
                        te.matmul(psC[:, 4:6], wu_bf[par][:, kc * 128:(kc + 1) * 128],
                                  sb_lnbb[:, kc:kc + 2],
                                  start=(kc == 0), stop=(kc == 7))
                te.sem_inc(sems["st"], 1)
            st_bias = nv("st")

            @block.vector
            def _(v, ec=ec, st_bias=st_bias):
                v.wait_ge(sems["st"], st_bias)
                if P1B:
                    v.tensor_copy(bias_u[:], psC[:, 4:6])
                else:
                    v.memset(bias_u[:], 0.0)
                v.sem_inc(sems["sv"], 1)
            sv_bias[ec] = nv("sv")

            for ri, (c0, w) in enumerate(rounds):
                ps = psA if ri % 2 == 0 else psB

                @block.tensor
                def _(te, ec=ec, par=par, ri=ri, c0=c0, w=w, ps=ps):
                    if ri == 0 and ec > 0:
                        te.wait_ge(sems["sv"], sv_uev[(ec - 1, 4)])
                        te.wait_ge(sems["st"], st_wxp.get(ec - 1, 0))
                    if ri >= 2:
                        te.wait_ge(sems["sv"], sv_uev[(ec, ri - 2)])
                    src = xmov[0] if c0 < 1024 else xmov[1]
                    cc0 = c0 if c0 < 1024 else c0 - 1024
                    if P1U:
                        for kc in range(8):
                            te.matmul(ps[:, 0:w], wu_bf[par][:, kc * 128:(kc + 1) * 128],
                                      src[:, kc * 1028 + cc0:kc * 1028 + cc0 + w],
                                      start=(kc == 0), stop=(kc == 7))
                    te.sem_inc(sems["st"], 1)
                st_u = nv("st")
                if ri == len(rounds) - 1:
                    st_u_last[ec] = st_u

                @block.vector
                def _(v, ec=ec, ri=ri, c0=c0, w=w, ps=ps, st_u=st_u):
                    v.wait_ge(sems["st"], st_u)
                    if ri == 0:
                        v.wait_ge(sems["sv"], sv_bias[ec])
                        if ec > 0:
                            v.wait_ge(sems["st"], st_conv_last[0])
                    if P1U:
                        v.tensor_scalar(u_ec[:, c0:c0 + w], ps[:, 0:w], bias_u[:, 0:1],
                                        None, AL.add)
                    v.sem_inc(sems["sv"], 1)
                sv_uev[(ec, ri)] = nv("sv")

            @block.vector
            def _(v, ec=ec):
                v.wait_ge(sems["sv"], sv_uev[(ec, 0)])
                v.tensor_scalar(scr3[:, 0:3], u_ec[:, 0:3], sb_pfx[:, 0:1], None,
                                AL.mult)
                v.drain(fusable=False)
                v.tensor_copy(u_ec[:, 0:3], scr3[:, 0:3])
                for k in range(4):
                    v.tensor_scalar(diag_k[:, k * 128:(k + 1) * 128], iden[:],
                                    sb_convw[:, ec * 4 + k:ec * 4 + k + 1], None,
                                    AL.mult)
                v.sem_inc(sems["sv"], 1)
            sv_dm = nv("sv")

            for ri in range(4):
                ps = psC if ri % 2 == 0 else psD

                @block.tensor
                def _(te, ec=ec, ri=ri, ps=ps):
                    if ri == 0:
                        te.wait_ge(sems["sv"], sv_dm)
                        te.wait_ge(sems["sv"], sv_uev[(ec, 4)])
                    if ri >= 2:
                        te.wait_ge(sems["ss"], ss_silu[(ec, ri - 2)])
                    elif ec > 0:
                        te.wait_ge(sems["ss"], ss_silu[(ec - 1, ri + 2)])
                    if P1C:
                        for k in range(4):
                            te.matmul(ps[:, 0:512], diag_k[:, k * 128:(k + 1) * 128],
                                      u_ec[:, ri * 512 + k:ri * 512 + k + 512],
                                      start=(k == 0), stop=(k == 3))
                    te.sem_inc(sems["st"], 1)
                st_conv = nv("st")
                if ri == 3:
                    st_conv_last[0] = st_conv

                @block.scalar
                def _(s, ec=ec, ri=ri, ps=ps, st_conv=st_conv):
                    s.wait_ge(sems["st"], st_conv)
                    if ri == 0 and ec > 0:
                        s.wait_ge(sems["dsp"], dsp_u2[ec - 1])
                        s.wait_ge(sems["st"], st_wxp[ec - 1])
                    if P1C:
                        s.activation(u2_ec[:, ri * 512:(ri + 1) * 512], ps[:, 0:512],
                                     ACT.Silu, bias=sb_convb[:, ec:ec + 1], scale=1.0)
                    s.sem_inc(sems["ss"], 1)
                ss_silu[(ec, ri)] = nv("ss")

            @block.tensor
            def _(te, ec=ec):
                te.wait_ge(sems["ss"], ss_silu[(ec, 3)])
                te.wait_ge(sems["dgi"], d_wxp)
                if P1X:
                    for ls in range(4):
                        te.matmul(psBig[0:96, ls * 512:(ls + 1) * 512],
                                  wxp_bf[:, ec * 96:(ec + 1) * 96],
                                  u2_ec[:, ls * 512:(ls + 1) * 512],
                                  start=(ec == 0), stop=(ec == 15))
                te.sem_inc(sems["st"], 1)
            st_wxp[ec] = nv("st")

            @block.sync
            def _(sync, ec=ec):
                sync.wait_ge(sems["ss"], ss_silu[(ec, 3)])
                if P1S:
                    tgt = u2lo if ec < 8 else u2hi
                    ecl = ec % 8
                    sync.dma_start(out=tgt.ap()[ecl * 128:(ecl + 1) * 128, :],
                                   in_=u2_ec[:]).then_inc(sems["dsp"], 16)
                else:
                    sync.sem_inc(sems["dsp"], 16)
            dsp_u2[ec] = nd("dsp")

        @block.vector
        def _(v):
            if RUN_P1A and P1X:
                v.wait_ge(sems["st"], st_wxp[15])
                v.tensor_copy(xdbl_sb[:], psBig[0:96, :])
            v.sem_inc(sems["sv"], 1)
        sv_xdbl = nv("sv")

        @block.sync
        def _(sync):
            sync.wait_ge(sems["sv"], sv_xdbl)
            if RUN_P1A and P1X:
                sync.dma_start(out=xdbl_in.ap(), in_=xdbl_sb[:]).then_inc(sems["dsp"], 16)
            else:
                sync.sem_inc(sems["dsp"], 16)
        d_xdbl = nd("dsp")

        @block.gpsimd
        def _(g):
            g.wait_ge(sems["dsp"], d_xdbl)
            g.collective_compute("AllGather", AL.bypass, replica_groups=PAIRS,
                                 ins=[xdbl_in.ap().opt()],
                                 outs=[xdbl_out.ap().opt()]).then_inc(sems["cc"], 1)
        cc_xdbl = nv("cc")

        @block.gpsimd
        def _(g):
            g.wait_ge(sems["dsp"], max(dsp_u2[e] for e in range(8)) if RUN_P1A else 0)
            g.collective_compute("AllGather", AL.bypass, replica_groups=PAIRS,
                                 ins=[u2lo.ap().opt()],
                                 outs=[u2ag.ap()[0:2048, :].opt()]).then_inc(sems["cc"], 1)
        nv("cc")

        @block.gpsimd
        def _(g):
            g.wait_ge(sems["dsp"], max(dsp_u2[e] for e in range(8, 16)) if RUN_P1A else 0)
            g.collective_compute("AllGather", AL.bypass, replica_groups=PAIRS,
                                 ins=[u2hi.ap().opt()],
                                 outs=[u2ag.ap()[2048:4096, :].opt()]).then_inc(sems["cc"], 1)
        cc_u2 = nv("cc")

        barrier()

        @block.gpsimd
        def _(g):
            if not RUN_FFN:
                return
            for kc in range(8):
                g.dma_start(out=wq_bf[:, kc * D:(kc + 1) * D], in_=wqt[kc]).then_inc(
                    sems["dgi"], 16)
        if RUN_FFN:
            cnt["dgi"] += 16 * 8
        d_wq = cnt["dgi"]

        ffn_ev = {}
        ffn_mm_last = [0]
        ffn_gsp = {}
        ffn_q = {}
        ffn_tp = {}
        ffn_gtp = {}
        ffn_gev_last = {}
        ffn_qev_last = {}
        ffn_cd_last = [0]
        for blc in range(4 if RUN_FFN else 0):
            for tt in range(4):
                lt = blc * 4 + tt

                @block.sync
                def _(sync, lt=lt, blc=blc, tt=tt):
                    if blc + tt > 0:
                        sync.wait_ge(sems["sv"], ffn_q.get((blc, tt - 1) if tt > 0
                                                           else (blc - 1, 3), 0))
                    sync.dma_start(out=fx[:], in_=x3[3 + lt * 128:3 + (lt + 1) * 128, :]
                                   ).then_inc(sems["din"], 16)
                d_fx = nd("din")

                @block.vector
                def _(v, lt=lt, blc=blc, tt=tt, d_fx=d_fx):
                    v.wait_ge(sems["din"], d_fx)
                    if blc + tt > 0:
                        v.wait_ge(sems["st"], ffn_tp.get((blc, tt - 1) if tt > 0
                                                         else (blc - 1, 3), 0))
                    v.tensor_mul(kcol[:, 0:1], rs1s[:, lt:lt + 1], invs_c[:])
                    v.drain(fusable=False)
                    v.tensor_scalar(fx[:], fx[:], negmu[:, lt:lt + 1], kcol[:, 0:1],
                                    AL.add, AL.mult)
                    v.tensor_scalar(fq[:], fx[:], MAGIC, MAGIC, AL.add, AL.subtract)
                    v.sem_inc(sems["sv"], 1)
                ffn_q[(blc, tt)] = nv("sv")

                @block.tensor
                def _(te, blc=blc, tt=tt):
                    te.wait_ge(sems["sv"], ffn_q[(blc, tt)])
                    if ffn_cd_last[0]:
                        te.wait_ge(sems["sv"], ffn_cd_last[0])
                    for dc in range(8):
                        te.transpose(psCb[:, dc * 128:(dc + 1) * 128],
                                     fq[:, dc * 128:(dc + 1) * 128],
                                     iden[:]).then_inc(sems["st"], 1 if dc == 7 else 0)
                cnt["st"] += 1
                ffn_tp[(blc, tt)] = cnt["st"]

                @block.vector
                def _(v, blc=blc, tt=tt):
                    v.wait_ge(sems["st"], ffn_tp[(blc, tt)])
                    for dc in range(8):
                        v.tensor_copy(fqT[:, dc * 512 + tt * 128:dc * 512 + (tt + 1) * 128],
                                      psCb[:, dc * 128:(dc + 1) * 128])
                    v.sem_inc(sems["sv"], 1)
                ffn_cd_last[0] = nv("sv")
            sv_qready = cnt["sv"]

            for oc in range(8):
                ps = psA if oc % 2 == 0 else psB

                @block.tensor
                def _(te, oc=oc, ps=ps, blc=blc, sv_qready=sv_qready):
                    te.wait_ge(sems["sv"], sv_qready)
                    te.wait_ge(sems["dgi"], d_wq)
                    if oc >= 2:
                        te.wait_ge(sems["ss"], ffn_ev[(blc, oc - 2)])
                    elif blc > 0:
                        te.wait_ge(sems["ss"], ffn_ev[(blc - 1, oc + 6)])
                    for kc in range(8):
                        te.matmul(ps[:, 0:512],
                                  wq_bf[:, kc * D + oc * 128:kc * D + (oc + 1) * 128],
                                  fqT[:, kc * 512:(kc + 1) * 512],
                                  start=(kc == 0), stop=(kc == 7))
                    te.sem_inc(sems["st"], 1)
                st_f = nv("st")
                if oc == 7:
                    ffn_mm_last[0] = st_f

                @block.scalar
                def _(s, oc=oc, ps=ps, blc=blc, st_f=st_f):
                    s.wait_ge(sems["st"], st_f)
                    if (blc, oc) != (0, 0):
                        key = (blc, oc - 1) if oc > 0 else (blc - 1, 7)
                        s.wait_ge(sems["st"], ffn_gtp[key])
                    s.activation(fg[:], ps[:, 0:512], ACT.Silu,
                                 bias=sb_fb[:, oc:oc + 1], scale=sb_wscx[:, oc:oc + 1])
                    s.sem_inc(sems["ss"], 1)
                ffn_ev[(blc, oc)] = nv("ss")

                @block.tensor
                def _(te, blc=blc, oc=oc):
                    te.wait_ge(sems["ss"], ffn_ev[(blc, oc)])
                    if ffn_cd_last[0]:
                        te.wait_ge(sems["sv"], ffn_cd_last[0])
                    for sub in range(4):
                        te.transpose(psDb[:, sub * 128:(sub + 1) * 128],
                                     fg[:, sub * 128:(sub + 1) * 128],
                                     iden[:]).then_inc(sems["st"], 1 if sub == 3 else 0)
                cnt["st"] += 1
                ffn_gtp[(blc, oc)] = cnt["st"]

                @block.vector
                def _(v, blc=blc, oc=oc):
                    v.wait_ge(sems["st"], ffn_gtp[(blc, oc)])
                    if (blc, oc) != (0, 0):
                        key = (blc, oc - 1) if oc > 0 else (blc - 1, 7)
                        v.wait_ge(sems["dsp"], ffn_gsp[key])
                    for sub in range(4):
                        v.tensor_copy(fgT[:, sub * 128:(sub + 1) * 128],
                                      psDb[:, sub * 128:(sub + 1) * 128])
                    v.sem_inc(sems["sv"], 1)
                ffn_cd_last[0] = nv("sv")
                gev = [cnt["sv"]]

                @block.sync
                def _(sync, blc=blc, oc=oc, gv=ffn_cd_last[0]):
                    sync.wait_ge(sems["sv"], gv)
                    for sub in range(4):
                        sync.dma_start(
                            out=g_dram.ap()[blc * 512 + sub * 128: blc * 512 + (sub + 1) * 128,
                                            oc * 128:(oc + 1) * 128],
                            in_=fgT[:, sub * 128:(sub + 1) * 128]).then_inc(
                            sems["dsp"], 16)
                ffn_gsp[(blc, oc)] = cnt["dsp"] + 64
                cnt["dsp"] += 64
        ffn_gsp_all = cnt["dsp"]

        barrier()

        @block.gpsimd
        def _(g):
            if not RUN_SCAN:
                return
            g.wait_ge(sems["cc"], cc_xdbl)
            for s in range(2):
                g.dma_start(out=dtlo_bf[s][:],
                            in_=xdbl_out.ap()[s * 96:s * 96 + 64, :]).then_inc(
                    sems["dgi"], 16)
                g.dma_start(out=bc_bf[s][:],
                            in_=xdbl_out.ap()[s * 96 + 64:(s + 1) * 96, :]).then_inc(
                    sems["dgi"], 16)
            for ecl in range(8):
                g.dma_start(out=wdt_bf[:, ecl * 128:(ecl + 1) * 128],
                            in_=wdt[ecl]).then_inc(sems["dgi"], 16)
            for ecl in range(8):
                g.dma_start(out=wz_bf[:, ecl * 1024:(ecl + 1) * 1024]
                            .rearrange("p (kc c) -> p kc c", kc=8),
                            in_=winz[ecl].rearrange("kc p c -> p kc c")).then_inc(
                    sems["dgi"], 16)
        if RUN_SCAN:
            cnt["dgi"] += 16 * 20
        d_scan = cnt["dgi"]

        @block.tensor
        def _(te):
            if not RUN_SCAN:
                te.sem_inc(sems["st"], 1)
                return
            te.wait_ge(sems["dgi"], d_scan)
            for ecl in range(8):
                for kc in range(8):
                    te.matmul(psC[:, 8 + ecl:9 + ecl],
                              wz_bf[:, ecl * 1024 + kc * 128:ecl * 1024 + (kc + 1) * 128],
                              sb_lnbb[:, kc:kc + 1], start=(kc == 0), stop=(kc == 7))
            te.sem_inc(sems["st"], 1)
        st_bz = nv("st")

        @block.vector
        def _(v):
            v.wait_ge(sems["st"], st_bz)
            if RUN_SCAN:
                v.tensor_copy(bz_all[:], psC[:, 8:16]).then_inc(sems["sv"], 1)
            else:
                v.sem_inc(sems["sv"], 1)
        sv_bz = nv("sv")

        hreg_holder = [None]

        @block.sync
        def _(sync):
            sync.wait_ge(sems["din"], d_c0)
            hreg_holder[0] = sync.value_load(sb_hsc[0:1, 0:1])
            sync.wait_ge(sems["cc"], cc_u2)
            sync.dma_start(out=u2my.ap(),
                           in_=u2ag.ap()[bass.ts(hreg_holder[0], 2048), :],
                           bounds_check=None).then_inc(sems["dsp"], 16)

        d_u2my = nd("dsp")

        sc_dtev = {}
        sc_sp = {}
        sc_vd = {}
        sc_zmm = {}
        sc_zv = {}
        sc_scv = {}
        sc_ysp = {}
        sc_yv = {}
        d_xn2tg = {}
        d_u2 = {}

        for stg in range(4 if RUN_SCAN else 0):
            s2, so = stg >> 1, stg & 1
            hreg = hreg_holder[0]

            @block.sync
            def _(sync, stg=stg, s2=s2, so=so):
                sync.wait_ge(sems["cc"], cc_xn2t)
                if stg > 0:
                    sync.wait_ge(sems["st"], sc_zmm[(stg - 1, 7)])
                for dc in range(8):
                    sync.dma_start(
                        out=xn2tg[:, dc * 1024:(dc + 1) * 1024],
                        in_=xn2t_ag_out.ap()[s2 * 1024 + dc * 128:s2 * 1024 + (dc + 1) * 128,
                                             so * 1024:(so + 1) * 1024]).then_inc(
                        sems["din"], 16)
            cnt["din"] += 16 * 8
            d_xn2tg[stg] = cnt["din"]

            for ecl in range(8):
                @block.sync
                def _(sync, stg=stg, s2=s2, so=so, ecl=ecl):
                    sync.wait_ge(sems["dsp"], d_u2my)
                    prev = sc_vd.get((stg, ecl - 1) if ecl > 0 else (stg - 1, 7), 0)
                    sync.wait_ge(sems["sv"], prev)
                    r0 = s2 * 1024 + ecl * 128
                    sync.dma_start(
                        out=u2sc[:],
                        in_=u2my.ap()[r0:r0 + 128,
                                      so * 1024:(so + 1) * 1024]).then_inc(sems["din"], 16)
                d_u2v = nd("din")

                for half in range(2):
                    ps = psA if half == 0 else psB

                    @block.tensor
                    def _(te, stg=stg, s2=s2, so=so, ecl=ecl, half=half, ps=ps):
                        te.wait_ge(sems["dgi"], d_scan)
                        if half == 0:
                            if ecl > 0:
                                te.wait_ge(sems["sv"], sc_dtev[(stg, ecl - 1)])
                            elif stg > 0:
                                te.wait_ge(sems["ss"], sc_zv[(stg - 1, 7)])
                        te.matmul(ps[:, 0:512],
                                  wdt_bf[:, ecl * 128:(ecl + 1) * 128],
                                  dtlo_bf[s2][:, so * 1024 + half * 512:
                                              so * 1024 + (half + 1) * 512],
                                  start=True, stop=True)
                        te.sem_inc(sems["st"], 1)
                    nv("st")
                st_dt = cnt["st"]

                @block.vector
                def _(v, stg=stg, ecl=ecl, st_dt=st_dt):
                    v.wait_ge(sems["st"], st_dt)
                    v.tensor_scalar(dt_all[:, ecl * 1024:ecl * 1024 + 512], psA[:, 0:512],
                                    sb_bdt[:, ecl:ecl + 1], None, AL.add)
                    v.tensor_scalar(dt_all[:, ecl * 1024 + 512:(ecl + 1) * 1024],
                                    psB[:, 0:512], sb_bdt[:, ecl:ecl + 1], None, AL.add)
                    v.sem_inc(sems["sv"], 1)
                sc_dtev[(stg, ecl)] = nv("sv")

                @block.scalar
                def _(s, stg=stg, ecl=ecl):
                    s.wait_ge(sems["sv"], sc_dtev[(stg, ecl)])
                    seg = dt_all[:, ecl * 1024:(ecl + 1) * 1024]
                    s.activation(sp_tmp[:], seg, ACT.Exp)
                    s.activation(seg, sp_tmp[:], ACT.Ln, bias=1.0, scale=1.0)
                    s.sem_inc(sems["ss"], 1)
                sc_sp[(stg, ecl)] = nv("ss")

                @block.vector
                def _(v, stg=stg, ecl=ecl, d_u2v=d_u2v):
                    v.wait_ge(sems["ss"], sc_sp[(stg, ecl)])
                    v.wait_ge(sems["din"], d_u2v)
                    seg = dt_all[:, ecl * 1024:(ecl + 1) * 1024]
                    v.tensor_mul(v_all[:, ecl * 1024:(ecl + 1) * 1024], seg, u2sc[:])
                    v.tensor_scalar(y_all[:, ecl * 1024:(ecl + 1) * 1024], u2sc[:],
                                    sb_dpar[:, ecl:ecl + 1], None, AL.mult)
                    v.sem_inc(sems["sv"], 1)
                sc_vd[(stg, ecl)] = nv("sv")

            for ecl in range(8):
                for half in range(2):
                    ps = psA if half == 0 else psB

                    @block.tensor
                    def _(te, stg=stg, ecl=ecl, half=half, ps=ps):
                        if half == 0:
                            if ecl == 0:
                                te.wait_ge(sems["sv"], sc_vd[(stg, 7)])
                                te.wait_ge(sems["sv"], sv_bz)
                                te.wait_ge(sems["din"], d_xn2tg[stg])
                            else:
                                te.wait_ge(sems["ss"], sc_zv[(stg, ecl - 1)])
                        for kc in range(8):
                            te.matmul(ps[:, 0:512],
                                      wz_bf[:, ecl * 1024 + kc * 128:ecl * 1024 + (kc + 1) * 128],
                                      xn2tg[:, kc * 1024 + half * 512:kc * 1024 + (half + 1) * 512],
                                      start=(kc == 0), stop=(kc == 7))
                        te.sem_inc(sems["st"], 1)
                    nv("st")
                sc_zmm[(stg, ecl)] = cnt["st"]

                @block.scalar
                def _(s, stg=stg, ecl=ecl):
                    s.wait_ge(sems["st"], sc_zmm[(stg, ecl)])
                    if stg == 0 and ecl == 0:
                        s.wait_ge(sems["dsp"], dsp_u2[15])
                    s.activation(sz_all[:, ecl * 1024:ecl * 1024 + 512], psA[:, 0:512],
                                 ACT.Silu, bias=bz_all[:, ecl:ecl + 1], scale=1.0)
                    s.activation(sz_all[:, ecl * 1024 + 512:(ecl + 1) * 1024],
                                 psB[:, 0:512], ACT.Silu, bias=bz_all[:, ecl:ecl + 1],
                                 scale=1.0)
                    s.sem_inc(sems["ss"], 1)
                sc_zv[(stg, ecl)] = nv("ss")

            for n in range(N):
                @block.tensor
                def _(te, stg=stg, s2=s2, so=so, n=n):
                    if n > 0:
                        te.wait_ge(sems["sv"], sc_scv[(stg, (n - 1) * 8 + 7)])
                    else:
                        te.wait_ge(sems["sv"], sc_vd[(stg, 7)])
                        if stg > 0:
                            pass
                    for half in range(2):
                        te.matmul(psBig[:, half * 512:(half + 1) * 512],
                                  rowsel[:, n * 128:(n + 1) * 128],
                                  bc_bf[s2][:, so * 1024 + half * 512:
                                            so * 1024 + (half + 1) * 512],
                                  start=True, stop=True)
                        te.matmul(psBig[:, 1024 + half * 512:1024 + (half + 1) * 512],
                                  rowsel[:, (16 + n) * 128:(17 + n) * 128],
                                  bc_bf[s2][:, so * 1024 + half * 512:
                                            so * 1024 + (half + 1) * 512],
                                  start=True, stop=True)
                    te.sem_inc(sems["st"], 1)
                st_bc = nv("st")

                for ecl in range(8):
                    k = n * 8 + ecl
                    par = k % 2
                    col = ecl * 16 + n

                    @block.scalar
                    def _(s, stg=stg, n=n, ecl=ecl, par=par, col=col, k=k):
                        if k >= 2:
                            s.wait_ge(sems["sv"], sc_scv[(stg, k - 2)])
                        s.activation(dA_t[par][:], dt_all[:, ecl * 1024:(ecl + 1) * 1024],
                                     ACT.Exp, bias=0.0, scale=a_sc[:, col:col + 1])
                        s.sem_inc(sems["ss"], 1)
                    s_dA = nv("ss")

                    @block.vector
                    def _(v, stg=stg, n=n, ecl=ecl, par=par, col=col, s_dA=s_dA,
                          st_bc=st_bc):
                        v.wait_ge(sems["st"], st_bc)
                        v.wait_ge(sems["ss"], s_dA)
                        v.tensor_mul(w_t[:], v_all[:, ecl * 1024:(ecl + 1) * 1024],
                                     psBig[:, 0:1024])
                        init = 0.0 if stg == 0 else hst[:, col:col + 1]
                        v.tensor_tensor_scan(h_t[:], dA_t[par][:], w_t[:], init,
                                             AL.mult, AL.add)
                        v.tensor_mul(hc_t[:], h_t[:], psBig[:, 1024:2048])
                        v.tensor_copy(hst[:, col:col + 1], h_t[:, 1023:1024])
                        v.tensor_add(y_all[:, ecl * 1024:(ecl + 1) * 1024],
                                     y_all[:, ecl * 1024:(ecl + 1) * 1024], hc_t[:])
                        v.sem_inc(sems["sv"], 1)
                    sc_scv[(stg, k)] = nv("sv")

            for ecl in range(8):
                @block.vector
                def _(v, stg=stg, ecl=ecl):
                    v.wait_ge(sems["sv"], sc_scv[(stg, N * 8 - 1)])
                    if (stg, ecl - 1) in sc_yv or stg > 0:
                        key = (stg, ecl - 1) if ecl > 0 else (stg - 1, 7)
                        v.wait_ge(sems["dsp"], sc_ysp.get(key, 0))
                    v.tensor_mul(yec_bf[:], y_all[:, ecl * 1024:(ecl + 1) * 1024],
                                 sz_all[:, ecl * 1024:(ecl + 1) * 1024])
                    v.sem_inc(sems["sv"], 1)
                sc_yv[(stg, ecl)] = nv("sv")

                @block.sync
                def _(sync, stg=stg, s2=s2, so=so, ecl=ecl):
                    sync.wait_ge(sems["sv"], sc_yv[(stg, ecl)])
                    tgt = ya_in if s2 == 0 else yb_in
                    sync.dma_start(out=tgt.ap()[ecl * 128:(ecl + 1) * 128,
                                                so * 1024:(so + 1) * 1024],
                                   in_=yec_bf[:]).then_inc(sems["dsp"], 16)
                sc_ysp[(stg, ecl)] = nd("dsp")

        @block.gpsimd
        def _(g):
            g.wait_ge(sems["dsp"], sc_ysp.get((1, 7), 0))
            g.collective_compute("AllGather", AL.bypass, replica_groups=PAIRS,
                                 ins=[ya_in.ap().opt()],
                                 outs=[yag.ap()[0:2048, :].opt()]).then_inc(sems["cc"], 1)
        nv("cc")

        @block.gpsimd
        def _(g):
            g.wait_ge(sems["dsp"], sc_ysp.get((3, 7), 0))
            g.collective_compute("AllGather", AL.bypass, replica_groups=PAIRS,
                                 ins=[yb_in.ap().opt()],
                                 outs=[yag.ap()[2048:4096, :].opt()]).then_inc(sems["cc"], 1)
        cc_y = nv("cc")

        @block.sync
        def _(sync):
            sync.wait_ge(sems["cc"], cc_y)
            sync.dma_start(out=ymy.ap(),
                           in_=yag.ap()[bass.ts(hreg_holder[0], 2048), :],
                           bounds_check=None).then_inc(sems["dsp"], 16)
        d_ymy = nd("dsp")

        barrier()

        @block.gpsimd
        def _(g):
            if not RUN_WOUT:
                return
            g.wait_ge(sems["dsp"], ffn_gsp_all)
            for ec in range(16):
                g.dma_start(out=wo_bf[:, ec * D:(ec + 1) * D], in_=wout[ec]).then_inc(
                    sems["dgi"], 16)
        if RUN_WOUT:
            cnt["dgi"] += 16 * 16
        d_wo = cnt["dgi"]

        wo_mm = {}
        wo_as = {}
        wo_osp = {}
        for blc in range(4 if RUN_WOUT else 0):
            hreg = hreg_holder[0]

            @block.sync
            def _(sync, blc=blc):
                sync.wait_ge(sems["dsp"], d_ymy)
                if blc > 0:
                    sync.wait_ge(sems["st"], wo_mm[(blc - 1, 3)])
                for ec in range(16):
                    sync.dma_start(out=yst[:, ec * 512:(ec + 1) * 512],
                                   in_=ymy.ap()[ec * 128:(ec + 1) * 128,
                                                blc * 512:(blc + 1) * 512]).then_inc(
                        sems["din"], 16)
            cnt["din"] += 16 * 16
            d_yst = cnt["din"]

            for sub in range(4):
                lc = blc * 4 + sub
                psO = [psA, psB] if sub % 2 == 0 else [psC, psD]

                @block.tensor
                def _(te, blc=blc, sub=sub, psO=psO, d_yst=d_yst):
                    te.wait_ge(sems["din"], d_yst)
                    te.wait_ge(sems["dgi"], d_wo)
                    idx = blc * 4 + sub
                    if idx >= 2:
                        te.wait_ge(sems["sv"], wo_as[idx - 2])
                    for oc in range(2):
                        for ec in range(16):
                            te.matmul(psO[oc][:, 0:512],
                                      yst[:, ec * 512 + sub * 128:ec * 512 + (sub + 1) * 128],
                                      wo_bf[:, ec * D + oc * 512:ec * D + (oc + 1) * 512],
                                      start=(ec == 0), stop=(ec == 15))
                    te.sem_inc(sems["st"], 1)
                wo_mm[(blc, sub)] = nv("st")

                @block.sync
                def _(sync, lc=lc, blc=blc, sub=sub):
                    idx = blc * 4 + sub
                    if idx > 0:
                        sync.wait_ge(sems["sv"], wo_as[idx - 1])
                    sync.wait_ge(sems["dsp"], ffn_gsp_all)
                    sync.dma_start(out=as_x[:], in_=x3[3 + lc * 128:3 + (lc + 1) * 128, :]
                                   ).then_inc(sems["din"], 16)
                    sync.dma_start(out=as_g[:], in_=g_dram.ap()[lc * 128:(lc + 1) * 128, :]
                                   ).then_inc(sems["din"], 16)
                cnt["din"] += 32
                d_as = cnt["din"]

                @block.vector
                def _(v, lc=lc, blc=blc, sub=sub, psO=psO, d_as=d_as):
                    idx = blc * 4 + sub
                    v.wait_ge(sems["din"], d_as)
                    v.wait_ge(sems["st"], wo_mm[(blc, sub)])
                    if idx >= 2:
                        v.wait_ge(sems["dou"], wo_osp[idx - 2])
                    v.tensor_scalar(as_xn[:], as_x[:], negmu[:, lc:lc + 1],
                                    rs1s[:, lc:lc + 1], AL.add, AL.mult)
                    for oc in range(2):
                        v.scalar_tensor_tensor(as_s[:, oc * 512:(oc + 1) * 512],
                                               psO[oc][:, 0:512], sb_res[:, 0:1],
                                               as_xn[:, oc * 512:(oc + 1) * 512],
                                               AL.mult, AL.add)
                    if KOUT == 0:
                        v.tensor_mul(as_s[:], as_s[:], as_g[:])
                        v.tensor_add(as_o[idx % 2][:], as_s[:], as_x[:])
                    elif KOUT == 1:
                        v.tensor_copy(as_o[idx % 2][:], as_g[:])
                    elif KOUT == 2:
                        v.tensor_copy(as_o[idx % 2][:], as_s[:])
                    elif KOUT == 3:
                        v.tensor_copy(as_o[idx % 2][:], as_xn[:])
                    elif KOUT == 4:
                        for oc in range(2):
                            v.tensor_copy(as_o[idx % 2][:, oc * 512:(oc + 1) * 512],
                                          psO[oc][:, 0:512])
                    elif KOUT == 5:
                        v.tensor_scalar(as_o[idx % 2][:], as_x[:], 0.0,
                                        rs1s[:, lc:lc + 1], AL.mult, AL.add)
                    elif KOUT == 6:
                        v.tensor_scalar(as_o[idx % 2][:], as_x[:], 0.0,
                                        negmu[:, lc:lc + 1], AL.mult, AL.add)
                    v.sem_inc(sems["sv"], 1)
                wo_as[blc * 4 + sub] = nv("sv")

                @block.sync
                def _(sync, lc=lc, blc=blc, sub=sub):
                    idx = blc * 4 + sub
                    sync.wait_ge(sems["sv"], wo_as[idx])
                    sync.dma_start(out=out_hbm[lc * 128:(lc + 1) * 128, :],
                                   in_=as_o[idx % 2][:]).then_inc(sems["dou"], 16)
                wo_osp[blc * 4 + sub] = nd("dou")

        @block.sync
        def _(sync):
            if not RUN_WOUT:
                for lc in range(16):
                    sync.dma_start(out=out_hbm[lc * 128:(lc + 1) * 128, :],
                                   in_=x3[3 + lc * 128:3 + (lc + 1) * 128, :]).then_inc(
                        sems["dou"], 16)
                    nd("dou")
            sync.wait_ge(sems["dou"], cnt["dou"])

    return nc


_NC_CACHE = {}


def kernel(L0_attn_x, L0_attn_ln_g, L0_attn_ln_b, L0_attn_W_in, L0_attn_conv_w,
           L0_attn_conv_b, L0_attn_W_xp, L0_attn_W_dt, L0_attn_b_dt, L0_attn_A_log,
           L0_attn_D_param, L0_attn_W_out, L0_attn_res_scale, L0_ffn_weight_int8,
           L0_ffn_weight_scale, L0_ffn_bias):
    x = np.ascontiguousarray(np.asarray(L0_attn_x, dtype=np.float32))
    W_in = np.asarray(L0_attn_W_in, np.float32)
    W_xp = np.asarray(L0_attn_W_xp, np.float32)
    W_dt = np.asarray(L0_attn_W_dt, np.float32)
    W_out = np.asarray(L0_attn_W_out, np.float32)
    conv_w = np.asarray(L0_attn_conv_w, np.float32)
    conv_b = np.asarray(L0_attn_conv_b, np.float32)
    b_dt = np.asarray(L0_attn_b_dt, np.float32)
    A_log = np.asarray(L0_attn_A_log, np.float32)
    D_par = np.asarray(L0_attn_D_param, np.float32)
    ln_g = np.asarray(L0_attn_ln_g, np.float32)
    ln_b = np.asarray(L0_attn_ln_b, np.float32)
    res = np.asarray(L0_attn_res_scale, np.float32)
    wq8 = np.asarray(L0_ffn_weight_int8, np.int8)
    wsc = np.asarray(L0_ffn_weight_scale, np.float32)
    fbias = np.asarray(L0_ffn_bias, np.float32)

    winu_t = np.ascontiguousarray(
        W_in[:, :2048].reshape(8, 128, 16, 128).transpose(2, 0, 1, 3))
    wxp_t = np.ascontiguousarray(W_xp.reshape(16, 128, 96))
    wout_t = np.ascontiguousarray(W_out.reshape(16, 128, 1024))
    wqt_t = np.ascontiguousarray(wq8.T.reshape(8, 128, 1024))
    convw_c = np.ascontiguousarray(
        conv_w.reshape(16, 128, 4).transpose(1, 0, 2).reshape(128, 64))
    convb_c = np.ascontiguousarray(conv_b.reshape(16, 128).T)
    lng_c = np.ascontiguousarray(ln_g.reshape(8, 128).T)
    lnb_c = np.ascontiguousarray(ln_b.reshape(8, 128).T)
    wsc_c = np.ascontiguousarray(wsc.reshape(8, 128).T)
    fb_c = np.ascontiguousarray(fbias.reshape(8, 128).T)
    res_c = np.ascontiguousarray(np.broadcast_to(res.reshape(1, 1), (128, 1)))
    idc = np.eye(128, dtype=np.float32).astype(ml_dtypes.bfloat16)
    rsel = np.repeat(np.eye(32, dtype=np.float32), 128, axis=1).astype(
        ml_dtypes.bfloat16)

    in_maps = []
    for c in range(8):
        b, h = c >> 1, c & 1
        start = h * 2048
        pre = np.zeros((3, D), np.float32) if h == 0 else x[b, 2045:2048]
        x3 = np.ascontiguousarray(
            np.concatenate([pre, x[b, start:start + 2048]], axis=0))
        zc = W_in[:, 2048 + h * 1024:2048 + (h + 1) * 1024]
        winz_t = np.ascontiguousarray(zc.reshape(8, 128, 8, 128).transpose(2, 0, 1, 3))
        wdt_t = np.ascontiguousarray(
            W_dt[:, h * 1024:(h + 1) * 1024].reshape(64, 8, 128).transpose(1, 0, 2))
        alog_own = A_log[h * 1024:(h + 1) * 1024]
        alog_c = np.ascontiguousarray(
            alog_own.reshape(8, 128, 16).transpose(1, 0, 2).reshape(128, 128))
        bdt_c = np.ascontiguousarray(b_dt[h * 1024:(h + 1) * 1024].reshape(8, 128).T)
        dpar_c = np.ascontiguousarray(D_par[h * 1024:(h + 1) * 1024].reshape(8, 128).T)
        in_maps.append(dict(
            x3=x3, winu=winu_t, winz=winz_t, wxp=wxp_t, wdt=wdt_t, wout=wout_t,
            wqt=wqt_t, alog_c=alog_c, bdt_c=bdt_c, dpar_c=dpar_c, convw_c=convw_c,
            convb_c=convb_c, lng_c=lng_c, lnb_c=lnb_c, wsc_c=wsc_c, fb_c=fb_c,
            res_c=res_c, pfx_c=np.full((128, 1), float(h), np.float32),
            hsc=np.array([[h]], np.int32), idc=idc, rsel=rsel,
        ))

    if "nc" not in _NC_CACHE:
        _NC_CACHE["nc"] = build_nc()
    res_run = run_bass_kernel_spmd(_NC_CACHE["nc"], in_maps, core_ids=list(range(8)))

    out = np.empty((4, LG, D), np.float32)
    for c in range(8):
        b, h = c >> 1, c & 1
        out[b, h * 2048:(h + 1) * 2048, :] = res_run.results[c]["out"]
    return out
